# version 04 (final)
# baseline (speedup 1.0000x reference)
import numpy as np
import ml_dtypes
import contextlib
import concourse.bass as bass
import concourse.mybir as mybir
from concourse.bass_utils import run_bass_kernel_spmd

F32 = mybir.dt.float32
BF16 = mybir.dt.bfloat16
AF = mybir.ActivationFunctionType
ALU = mybir.AluOpType
AX = mybir.AxisListType

D = 2048
S = 2048
NP = 4
PT = 512
NCH = 4
L = 128
NH = 64
HD = 64
INTER = 4096
KD = 16
GB = 32
HB = 34
OBK = 16
EBT = 64
FQ = 4
FT = 512
NELEM_FFN = 8192 * 2048


def build_program(debug=False):
    nc = bass.Bass("TRN2", target_bir_lowering=False)

    dram = {}

    def din(name, shape, dt=F32):
        dram[name] = nc.dram_tensor(name, list(shape), dt, kind="ExternalInput")
        return dram[name]

    din("xT", (D, S))
    din("wing", (GB, 128, 2048))
    din("winh", (HB, 128, 2048))
    din("wind", (128, 1024))
    din("wout", (OBK, 128, 4096))
    din("wu", (EBT, 128, 2048))
    din("wg", (EBT, 128, 2048))
    din("wd", (EBT, 128, 2048))
    din("a1u", (EBT, 128, 2048))
    din("a1g", (EBT, 128, 2048))
    din("a1d", (EBT, 128, 2048))
    din("convw", (128, HB * 4))
    din("convb", (128, HB))
    din("nw", (128, GB))
    din("dtb", (NH, 1))
    din("alog", (NH, 1))
    din("dvec", (1, NH))
    din("cumf", (128, 128))
    din("identf", (128, 128))
    din("identb", (128, 128), BF16)
    din("ones1f", (1, 128))
    din("ones1b", (1, 128), BF16)
    din("onescb", (128, 1), BF16)
    din("onescf", (128, 1))
    out_t = nc.dram_tensor("out", [D, S], F32, kind="ExternalOutput")
    dbg = {}
    if debug:
        for nm, shp, dt in [("d_acc0", (128, 8192), F32),
                            ("d_xq", (128, 8192), BF16),
                            ("d_hq", (128, 8192), BF16),
                            ("d_wq", (128, 2048), BF16),
                            ("d_xn", (128, 8192), BF16),
                            ("d_gate", (128, 16384), BF16),
                            ("d_hs", (128, 16384), BF16),
                            ("d_bc", (128, 1024), BF16),
                            ("d_dt", (NH, 512), F32),
                            ("d_af", (NH, 512), F32),
                            ("d_stm", (128, NH), F32),
                            ("d_cbt", (128, 128), F32),
                            ("d_st", (128, 4096), BF16),
                            ("d_y", (128, 16384), BF16),
                            ("d_yn", (128, 16384), BF16),
                            ("d_attn", (128, 8192), BF16),
                            ("d_x2n", (128, 8192), BF16),
                            ("d_cc", (128, 8), F32),
                            ("d_acc", (128, 8192), F32)]:
            dbg[nm] = nc.dram_tensor(nm, list(shp), dt, kind="ExternalOutput")

    def ddump(nm, ap):
        if debug:
            dma("gp", dbg[nm].ap()[:, :], ap, "o", waits=[W("v"), W("s"),
                                                          W("pe")])
    x2s = nc.dram_tensor("x2s", [D, S], F32, kind="Internal")
    x2n = nc.dram_tensor("x2n", [D, S], BF16, kind="Internal")
    wib = nc.dram_tensor("wib", [GB + HB, 128, 2048], BF16, kind="Internal")
    wob = nc.dram_tensor("wob", [OBK, 128, 4096], BF16, kind="Internal")
    qu = nc.dram_tensor("qu", [EBT, 128, 2048], BF16, kind="Internal")
    qg = nc.dram_tensor("qg", [EBT, 128, 2048], BF16, kind="Internal")
    qd = nc.dram_tensor("qd", [EBT, 128, 2048], BF16, kind="Internal")

    sbn = {}

    def A(name, shape, dt):
        sbn[name] = nc.alloc_sbuf_tensor(name, list(shape), dt)
        return sbn[name]

    xn_p = A("xn_p", [128, KD * PT], BF16)
    pool_a = A("pool_a", [128, 8192], F32)
    gate_p = pool_a.bitcast(BF16)
    hs_p = A("hs_p", [128, GB * PT], BF16)
    bc_p = A("bc_p", [128, 2 * PT], BF16)
    attn_p = A("attn_p", [128, KD * PT], BF16)
    dt_f = A("dt_f", [NH, PT], F32)
    a_f = A("a_f", [NH, PT], F32)
    s_tm = A("s_tm", [128, NH], F32)
    ns_tm = A("ns_tm", [128, NH], F32)
    dt_tm = A("dt_tm", [128, NH], F32)
    a_tm = A("a_tm", [128, NH], F32)
    s_fm = A("s_fm", [NH, 128], F32)
    s_hi = A("s_hi", [NH, 128], BF16)
    slo_f = A("slo_f", [NH, 128], F32)
    s_lo = A("s_lo", [NH, 128], BF16)
    srow_hi = A("srow_hi", [1, 4096], BF16)
    srow_lo = A("srow_lo", [1, 4096], BF16)
    Em2 = A("Em2", [128, 256], F32)
    Tt2 = A("Tt2", [128, 256], F32)
    ES2 = A("ES2", [128, 512], F32)
    MT2 = A("MT2", [128, 256], BF16)
    CT2 = A("CT2", [128, 256], BF16)
    XW2 = A("XW2", [128, 128], BF16)
    WC2 = A("WC2", [128, 2], F32)
    TG2 = A("TG2", [128, 512], BF16)
    YD2 = A("YD2", [128, 256], F32)
    SG2 = A("SG2", [128, 256], F32)
    CBTm = A("CBTm", [128, 128], F32)
    B_tmb = A("B_tmb", [128, 128], BF16)
    X_tm = A("X_tm", [128, INTER], BF16)
    ST = A("ST", [128, NH * HD], BF16)
    ext = A("ext", [128, PT + 4], F32)
    halo = A("halo", [128, HB * 3], F32)
    wf = [A("wf%d" % i, [128, 2048], F32) for i in range(2)]
    wb = [A("wb%d" % i, [128, 2048], BF16) for i in range(2)]
    qtmp = A("qtmp", [128, 2048], BF16)
    a1b0 = A("a1b0", [128, 2048], F32)
    a1b = [a1b0, a1b0]
    a1au = A("a1au", [128, 1], F32)
    a1ag = A("a1ag", [128, 1], F32)
    a1ad = A("a1ad", [128, 1], F32)
    a1tmp = A("a1tmp", [128, 1], F32)
    a1vtmp = A("a1vtmp", [128, 1], F32)
    a1vtmp2 = A("a1vtmp2", [128, 1], F32)
    partial = A("partial", [1, 4], F32)
    totals = A("totals", [1, 4], F32)
    row6 = A("row6", [1, 8], F32)
    cc = A("cc", [128, 8], F32)
    ncc = A("ncc", [128, 4], F32)
    xb2 = [A("xb%d" % i, [128, 512], F32) for i in range(2)]
    sq2 = [A("sq%d" % i, [128, 512], BF16) for i in range(4)]
    rows = A("rows", [1, 1024], F32)
    convw_s = A("convw_s", [128, HB * 4], F32)
    halfb_s = A("halfb_s", [128, HB], F32)
    convb_s = A("convb_s", [128, HB], F32)
    nw_s = A("nw_s", [128, GB], F32)
    dtb_s = A("dtb_s", [NH, 1], F32)
    acol_s = A("acol_s", [NH, 1], F32)
    dsb = A("dsb", [1, NH], F32)
    dbc = A("dbc", [128, NH], F32)
    cum_s = A("cum_s", [128, 128], F32)
    idf_s = A("idf_s", [128, 128], F32)
    idb_s = A("idb_s", [128, 128], BF16)
    o1f = A("o1f", [1, 128], F32)
    o1b = A("o1b", [1, 128], BF16)
    ocb = A("ocb", [128, 1], BF16)
    ocf = A("ocf", [128, 1], F32)
    wdt_b = A("wdt_b", [128, 1024], BF16)
    t_f = [A("t_f%d" % i, [128, 512], F32) for i in range(2)]
    obf = t_f
    CONSTVALS = [1e-5, 1.0, -1.0, 0.5, -0.5, 1e-8, 2.0, 0.0]
    cst = A("cst", [128, len(CONSTVALS)], F32)
    for ci, cv in enumerate(CONSTVALS):
        nc.const_aps.register(F32, cv, cst[:, ci:ci + 1])

    pA0 = nc.alloc_psum_tensor("pA0", [128, 512], F32)
    pA1 = nc.alloc_psum_tensor("pA1", [128, 512], F32)
    pEr = nc.alloc_psum_tensor("pEr", [128, 512], F32)
    pY = nc.alloc_psum_tensor("pY", [128, 512], F32)
    pC = nc.alloc_psum_tensor("pC", [128, 512], F32)
    pTb = nc.alloc_psum_tensor("pTb", [128, 1024], BF16)
    pTf = nc.alloc_psum_tensor("pTf", [128, 512], F32)
    pTb2 = nc.alloc_psum_tensor("pTb2", [128, 1024], BF16)

    stack = contextlib.ExitStack()
    SEMS = ["w", "a", "s", "v", "pe", "pee", "c", "o", "q1", "q2"]
    sem = {k: stack.enter_context(nc.semaphore("sem_" + k)) for k in SEMS}
    C = {k: 0 for k in SEMS}
    ops = {k: [] for k in ["sync", "gp", "sc", "v", "pe"]}
    ENG = {}
    lastw = {k: {} for k in ops}

    def E(eng, fn, waits=(), inc=None, inc_n=1):
        rw = []
        for (sm, val) in waits:
            if val <= 0 or lastw[eng].get(sm, -1) >= val:
                continue
            lastw[eng][sm] = val
            rw.append((sm, val))
        ops[eng].append((rw, fn, inc, inc_n))
        if inc is not None:
            C[inc] += inc_n

    def W(sm):
        return (sm, C[sm])

    def dma(eng, out_ap, in_ap, smn, waits=()):
        E(eng, lambda o=out_ap, i=in_ap: ENG[eng].dma_start(out=o, in_=i),
          waits=waits, inc=smn, inc_n=16)

    def v_tt(out, i0, i1, op, waits=(), inc=None):
        E("v", lambda o=out, a=i0, b=i1, p=op: ENG["v"].tensor_tensor(o, a, b, p),
          waits=waits, inc=inc)

    def v_ts(out, i0, s1, op0, s2=None, op1=None, waits=(), inc=None):
        if s2 is None:
            E("v", lambda o=out, a=i0, s=s1, p=op0:
              ENG["v"].tensor_single_scalar(o, a, s, p), waits=waits, inc=inc)
        else:
            E("v", lambda o=out, a=i0, x=s1, y=s2, p=op0, q=op1:
              ENG["v"].tensor_scalar(o, a, x, y, p, q), waits=waits, inc=inc)

    def v_stt(out, i0, sc, i1, op0, op1, waits=(), inc=None):
        E("v", lambda o=out, a=i0, s=sc, b=i1, p=op0, q=op1:
          ENG["v"].scalar_tensor_tensor(o, a, s, b, p, q), waits=waits, inc=inc)

    def v_cp(out, i0, waits=(), inc=None):
        E("v", lambda o=out, a=i0: ENG["v"].tensor_copy(o, a), waits=waits,
          inc=inc)

    def s_act(out, i0, func, bias=0.0, scale=1.0, accum=None, waits=(),
              inc=None):
        E("sc", lambda o=out, a=i0, f=func, b=bias, s=scale, ac=accum:
          ENG["sc"].activation(o, a, f, bias=b, scale=s, accum_out=ac),
          waits=waits, inc=inc)

    def s_cp(out, i0, waits=(), inc=None):
        E("sc", lambda o=out, a=i0: ENG["sc"].copy(o, a), waits=waits, inc=inc)

    def mm(out, lhsT, rhs, start, stop, waits=(), inc=None):
        E("pe", lambda o=out, l=lhsT, r=rhs, st=start, sp=stop:
          ENG["pe"].matmul(o, l, r, start=st, stop=sp), waits=waits, inc=inc)

    def tr(out, in_, ident, waits=(), inc=None):
        E("pe", lambda o=out, a=in_, i=ident: ENG["pe"].transpose(o, a, i),
          waits=waits, inc=inc)

    for (dst, src) in [(cum_s, "cumf"), (idf_s, "identf"), (idb_s, "identb"),
                       (o1f, "ones1f"), (o1b, "ones1b"), (ocb, "onescb"),
                       (ocf, "onescf"), (convw_s, "convw"), (convb_s, "convb"),
                       (nw_s, "nw"), (dtb_s, "dtb"), (dsb, "dvec")]:
        dma("gp", dst[:, :], dram[src].ap()[:, :], "a")
    dma("gp", wf[0][:, 0:1024], dram["wind"].ap()[:, :], "a")
    dma("gp", acol_s[:, :], dram["alog"].ap()[:, :], "a")
    init_a = C["a"]
    E("v", lambda: ENG["v"].memset(ST[:, :], 0.0))
    E("v", lambda: ENG["v"].memset(halo[:, :], 0.0))
    E("v", lambda: ENG["v"].memset(partial[:, :], 0.0))
    for ci, cv in enumerate(CONSTVALS):
        E("v", lambda a=cst[:, ci:ci + 1], x=cv: ENG["v"].memset(a, x))
    v_ts(halfb_s[:, :], convb_s[:, :], 0.5, ALU.mult,
         waits=[("a", init_a)], inc="v")
    s_act(acol_s[:, :], acol_s[:, :], AF.Exp, waits=[("a", init_a)], inc="s")
    v_ts(acol_s[:, :], acol_s[:, :], -1.0, ALU.mult, waits=[W("s")], inc="v")
    mm(pEr[:, 0:NH], o1f[:, :], dsb[:, :], True, True,
       waits=[("a", init_a)], inc="pe")
    v_cp(dbc[:, :], pEr[:, 0:NH], waits=[W("pe")], inc="v")
    s_cp(wdt_b[:, :], wf[0][:, 0:1024], waits=[("a", init_a)], inc="s")
    E("sc", lambda: ENG["sc"].memzero(a1au[:, :]))
    E("sc", lambda: ENG["sc"].memzero(a1ag[:, :]), inc="s")
    E("v", lambda: ENG["v"].memset(a1ad[:, :], 0.0), inc="v")

    a1_state = {"un": 0, "gn": 0, "dn": 0}

    def emit_a1_unit():
        un, gn, dn = a1_state["un"], a1_state["gn"], a1_state["dn"]
        if un < EBT:
            blk = a1b[0]
            dma("sc", blk[:, :], dram["a1u"].ap()[un], "q1", waits=[W("v")])
            s_act(blk[:, :], blk[:, :], AF.Abs, accum=a1tmp[:, :],
                  waits=[("q1", C["q1"])])
            E("sc", lambda: ENG["sc"].add(a1au[:, :], a1tmp[:, :], a1au[:, :]),
              inc="s")
            a1_state["un"] += 1
        if gn < EBT:
            blk = a1b[0]
            dma("sc", blk[:, :], dram["a1g"].ap()[gn], "q1")
            s_act(blk[:, :], blk[:, :], AF.Abs, accum=a1tmp[:, :],
                  waits=[("q1", C["q1"])])
            E("sc", lambda: ENG["sc"].add(a1ag[:, :], a1tmp[:, :], a1ag[:, :]),
              inc="s")
            a1_state["gn"] += 1
        if dn < EBT:
            blk = a1b[1]
            dma("sc", blk[:, :], dram["a1d"].ap()[dn], "q2", waits=[W("v")])
            vt = a1vtmp if dn % 2 == 0 else a1vtmp2
            vp = a1vtmp2 if dn % 2 == 0 else a1vtmp
            E("v", lambda b=blk, t=vt: ENG["v"].tensor_reduce(
                t[:, :], b[:, :], AX.X, ALU.add,
                apply_absolute_value=True), waits=[("q2", C["q2"])])
            if dn > 0:
                v_tt(a1ad[:, :], a1ad[:, :], vp[:, :], ALU.add, inc="v")
            a1_state["dn"] += 1

    def ln_rows(nfeat, eps, subtract_mean=True):
        pewait = W("pe")
        mean_r = t_f[0][0:1, 0:512]
        msq_r = t_f[1][0:1, 0:512]
        if subtract_mean:
            v_ts(mean_r, pTf[0:1, :], 1.0 / nfeat, ALU.mult, waits=[pewait])
            v_tt(msq_r, mean_r, mean_r, ALU.mult)
            v_stt(rows[0:1, 0:512], pEr[0:1, :], 1.0 / nfeat,
                  msq_r, ALU.mult, ALU.subtract, inc="v")
        else:
            v_ts(rows[0:1, 0:512], pEr[0:1, :], 1.0 / nfeat, ALU.mult,
                 waits=[pewait], inc="v")
        s_act(rows[0:1, 0:512], rows[0:1, 0:512], AF.Ln, bias=eps,
              waits=[W("v")])
        s_act(rows[0:1, 0:512], rows[0:1, 0:512], AF.Exp, scale=-0.5, inc="s")
        if subtract_mean:
            v_tt(rows[0:1, 512:1024], mean_r, rows[0:1, 0:512],
                 ALU.mult, waits=[W("s")], inc="v")

    def bcast2(two=True):
        mm(pA0[:, :], o1f[:, :], rows[0:1, 0:512], True, True,
           waits=[W("v"), W("s")], inc=None if two else "pe")
        if two:
            mm(pA1[:, :], o1f[:, :], rows[0:1, 512:1024], True, True,
               inc="pe")

    for part in range(NP):
        t0 = part * PT
        first = part == 0
        for i in range(KD):
            xb = xb2[i % 2]
            dma("gp", xb[:, :],
                dram["xT"].ap()[i * 128:(i + 1) * 128, t0:t0 + PT], "a",
                waits=[W("s")])
            aw = ("a", C["a"])
            s_act(sq2[0][:, :], xb[:, :], AF.Copy, waits=[aw, W("pe")])
            s_act(sq2[1][:, :], xb[:, :], AF.Square, inc="s")
            mm(pTf[0:1, :], ocb[:, :], sq2[0][:, :], i == 0, i == KD - 1,
               waits=[W("s")])
            mm(pEr[0:1, :], ocb[:, :], sq2[1][:, :], i == 0, i == KD - 1,
               inc="pe")
        ln_rows(D, 1e-5, True)
        bcast2()
        for i in range(KD):
            xb = xb2[i % 2]
            dma("gp", xb[:, :],
                dram["xT"].ap()[i * 128:(i + 1) * 128, t0:t0 + PT], "a",
                waits=[W("v")])
            aw = ("a", C["a"])
            v_tt(t_f[0][:, :], xb[:, :], pA0[:, :], ALU.mult,
                 waits=[aw, W("pe")])
            v_tt(xn_p[:, i * PT:(i + 1) * PT], t_f[0][:, :], pA1[:, :],
                 ALU.subtract, inc="v")

        if first:
            ddump("d_xn", xn_p[:, :])
        NB = GB + HB

        def inp_dma(bi, extra=()):
            if bi >= NB:
                return
            is_g = bi < GB
            eb = bi if is_g else bi - GB
            src = (dram["wing"].ap()[eb] if is_g else dram["winh"].ap()[eb])
            dma("sync", wf[bi % 2][:, :], src, "w",
                waits=list(extra) + [W("s")])
            wafter[bi] = C["w"]

        wafter = {}
        inp_dma(0)
        inp_dma(1)
        for bi in range(NB):
            is_g = bi < GB
            eb = bi if is_g else bi - GB
            wbb = wb[bi % 2]
            s_cp(wbb[:, :], wf[bi % 2][:, :],
                 waits=[("w", wafter[bi]), W("pe"), ("a", C["a"])],
                 inc="s")
            mmw = [W("s"), W("v")]
            ps = pA0 if bi % 2 == 0 else pA1
            for k in range(KD):
                mm(ps[:, :], wbb[:, k * 128:(k + 1) * 128],
                   xn_p[:, k * PT:(k + 1) * PT], k == 0, k == KD - 1,
                   waits=mmw if k == 0 else (),
                   inc="pe" if k == KD - 1 else None)
            inp_dma(bi + 2)
            pew = W("pe")
            if is_g:
                v_cp(gate_p[:, eb * PT:(eb + 1) * PT], ps[:, :], waits=[pew],
                     inc="v")
            else:
                v_cp(ext[:, 0:3], halo[:, eb * 3:(eb + 1) * 3], waits=[pew])
                v_cp(ext[:, 3:3 + PT], ps[:, :])
                wc = convw_s
                v_ts(t_f[0][:, :], ext[:, 0:PT], wc[:, eb * 4:eb * 4 + 1],
                     ALU.mult)
                v_stt(t_f[0][:, :], ext[:, 1:1 + PT],
                      wc[:, eb * 4 + 1:eb * 4 + 2], t_f[0][:, :], ALU.mult,
                      ALU.add)
                v_stt(t_f[0][:, :], ext[:, 2:2 + PT],
                      wc[:, eb * 4 + 2:eb * 4 + 3], t_f[0][:, :], ALU.mult,
                      ALU.add)
                v_stt(t_f[0][:, :], ext[:, 3:3 + PT],
                      wc[:, eb * 4 + 3:eb * 4 + 4], t_f[0][:, :], ALU.mult,
                      ALU.add)
                v_ts(t_f[0][:, :], t_f[0][:, :], convb_s[:, eb:eb + 1],
                     ALU.add)
                v_cp(halo[:, eb * 3:(eb + 1) * 3], ext[:, PT:PT + 3],
                     inc="v")
                s_act(t_f[1][:, :], t_f[0][:, :], AF.Tanh, scale=0.5,
                      waits=[W("v")], inc="s")
                v_ts(t_f[1][:, :], t_f[1][:, :], 1.0, ALU.add, 0.5, ALU.mult,
                     waits=[W("s")])
                dst = (hs_p[:, eb * PT:(eb + 1) * PT] if eb < 32 else
                       bc_p[:, (eb - 32) * PT:(eb - 31) * PT])
                v_tt(dst, t_f[1][:, :], t_f[0][:, :], ALU.mult, inc="v")
            if first:
                emit_a1_unit()
        for k in range(KD):
            mm(pA0[0:NH, :], wdt_b[:, k * 64:(k + 1) * 64],
               xn_p[:, k * PT:(k + 1) * PT], k == 0, k == KD - 1,
               waits=[W("v"), W("s")] if k == 0 else (),
               inc="pe" if k == KD - 1 else None)
        pew = W("pe")
        s_act(dt_f[:, :], pA0[0:NH, :], AF.Abs, bias=dtb_s[:, :], waits=[pew])
        s_act(dt_f[:, :], dt_f[:, :], AF.Exp, scale=-1.0)
        s_act(dt_f[:, :], dt_f[:, :], AF.Ln, bias=1.0)
        s_act(a_f[:, :], pA0[0:NH, :], AF.Relu, bias=dtb_s[:, :], inc="s")
        v_tt(dt_f[:, :], dt_f[:, :], a_f[:, :], ALU.add, waits=[W("s")])
        v_ts(a_f[:, :], dt_f[:, :], acol_s[:, :], ALU.mult, inc="v")

        if first:
            ddump("d_gate", gate_p[:, :])
            ddump("d_hs", hs_p[:, :])
            ddump("d_bc", bc_p[:, :])
            ddump("d_dt", dt_f[:, :])
            ddump("d_af", a_f[:, :])
        for c in range(NCH):
            for b8 in range(4):
                ptb = pTb if b8 % 2 == 0 else pTb2
                for j in range(8):
                    eb = b8 * 8 + j
                    tr(ptb[:, j * 128:(j + 1) * 128],
                       hs_p[:, eb * PT + c * L:eb * PT + (c + 1) * L],
                       idb_s[:, :], waits=[W("v")] if j == 0 else (),
                       inc="pe" if j == 7 else None)
                v_cp(X_tm[:, b8 * 1024:(b8 + 1) * 1024], ptb[:, :],
                     waits=[W("pe")], inc="v")
            tr(pTb[:, 0:128], bc_p[:, c * L:(c + 1) * L], idb_s[:, :],
               waits=[W("v")], inc="pe")
            v_cp(B_tmb[:, :], pTb[:, 0:128], waits=[W("pe")], inc="v")
            mm(pC[:, 0:128], bc_p[:, c * L:(c + 1) * L],
               bc_p[:, PT + c * L:PT + (c + 1) * L], True, True, inc="pe")
            v_tt(CBTm[:, :], pC[:, 0:128], cum_s[:, :], ALU.mult,
                 waits=[W("pe")], inc="v")
            tr(pTf[:, 0:64], dt_f[:, c * L:(c + 1) * L], idf_s[0:64, 0:64],
               waits=[W("v")])
            tr(pTf[:, 64:128], a_f[:, c * L:(c + 1) * L], idf_s[0:64, 0:64],
               inc="pe")
            v_cp(dt_tm[:, :], pTf[:, 0:64], waits=[W("pe")])
            v_cp(a_tm[:, :], pTf[:, 64:128], inc="v")
            mm(pTf[:, 128:192], cum_s[:, :], a_tm[:, :], True, True,
               waits=[W("v")])
            mm(pTf[0:64, 192:320], a_tm[:, :], cum_s[:, :], True, True,
               inc="pe")
            pewc = W("pe")
            v_cp(s_tm[:, :], pTf[:, 128:192], waits=[pewc])
            v_cp(s_fm[:, :], pTf[0:64, 192:320])
            v_ts(ns_tm[:, :], pTf[:, 128:192], -1.0, ALU.mult)
            v_cp(s_hi[:, :], s_fm[:, :])
            v_tt(slo_f[:, :], pTf[0:64, 192:320], s_hi[:, :], ALU.subtract)
            v_cp(s_lo[:, :], slo_f[:, :], inc="v")
            if first and c == 0:
                ddump("d_stm", s_tm[:, :])
                ddump("d_cbt", CBTm[:, :])
            dma("gp", srow_hi[0:1, :], s_hi[0:32, :], "a", waits=[W("v")])
            dma("gp", srow_lo[0:1, :], s_lo[0:32, :], "a")
            srw = [("a", C["a"])]
            pee_at = {}

            def erow(hg):
                bank = pEr if hg % 2 == 0 else pTf
                off = (hg % 8) * 512
                mm(bank[:, :], o1b[:, :],
                   srow_hi[0:1, off:off + 512], True, False,
                   waits=srw + [W("s"), W("v")])
                mm(bank[:, :], o1b[:, :],
                   srow_lo[0:1, off:off + 512], False, True,
                   inc="pee")
                pee_at[hg] = C["pee"]

            erow(0)
            for i in range(NH + 3):
                if i % 4 == 0 and i // 4 + 1 <= 15:
                    if i // 4 + 1 == 8:
                        dma("gp", srow_hi[0:1, :], s_hi[32:64, :], "a",
                            waits=[("pee", pee_at[7])])
                        dma("gp", srow_lo[0:1, :], s_lo[32:64, :], "a")
                        srw = [("a", C["a"])]
                    erow(i // 4 + 1)
                if i < NH:
                    h = i
                    par = i % 2
                    par4 = i % 4
                    bank = pEr if (h // 4) % 2 == 0 else pTf
                    slot = h % 4
                    epsl = bank[:, slot * 128:(slot + 1) * 128]
                    v_ts(Em2[:, par * 128:(par + 1) * 128], epsl,
                         ns_tm[:, h:h + 1], ALU.add, 0.0, ALU.min,
                         waits=[("pee", pee_at[h // 4])], inc="v")
                    pb = (h % 2) * 64
                    s_act(Tt2[:, par * 128:(par + 1) * 128],
                          Em2[:, par * 128:(par + 1) * 128], AF.Exp,
                          waits=[W("v")])
                    s_act(ES2[:, par4 * 128:(par4 + 1) * 128], epsl, AF.Exp)
                    s_act(TG2[pb:pb + 64, par4 * 128:(par4 + 1) * 128],
                          gate_p[pb:pb + 64, (h // 2) * PT + c * L:
                                 (h // 2) * PT + (c + 1) * L],
                          AF.Tanh, scale=0.5, inc="s")
                if 1 <= i <= NH:
                    j = i - 1
                    par = j % 2
                    v_tt(WC2[:, par:par + 1],
                         Tt2[:, par * 128 + 127:par * 128 + 128],
                         dt_tm[:, j:j + 1], ALU.mult, waits=[W("s")])
                    v_stt(MT2[:, par * 128:(par + 1) * 128],
                          Tt2[:, par * 128:(par + 1) * 128],
                          dt_tm[:, j:j + 1], CBTm[:, :], ALU.mult, ALU.mult)
                    v_tt(CT2[:, par * 128:(par + 1) * 128],
                         bc_p[:, PT + c * L:PT + (c + 1) * L],
                         ES2[:, (j % 4) * 128:(j % 4 + 1) * 128], ALU.mult)
                    v_ts(XW2[:, par * 64:(par + 1) * 64],
                         X_tm[:, j * 64:(j + 1) * 64], WC2[:, par:par + 1],
                         ALU.mult, inc="v")
                    pb = (j % 2) * 64
                    bk = [pY, pA0, pA1][j % 3]
                    mm(bk[pb:pb + 64, 0:128],
                       X_tm[:, j * 64:(j + 1) * 64],
                       MT2[:, par * 128:(par + 1) * 128], True, False,
                       waits=[W("v")])
                    mm(bk[pb:pb + 64, 0:128], ST[:, j * 64:(j + 1) * 64],
                       CT2[:, par * 128:(par + 1) * 128], False, True)
                    mm(bk[:, 128:192], B_tmb[:, :],
                       XW2[:, par * 64:(par + 1) * 64], True, True, inc="pe")
                if i >= 3:
                    j2 = i - 3
                    par = j2 % 2
                    pb = (j2 % 2) * 64
                    bk2 = [pY, pA0, pA1][j2 % 3]
                    gsl = gate_p[pb:pb + 64, (j2 // 2) * PT + c * L:
                                 (j2 // 2) * PT + (c + 1) * L]
                    hsl = hs_p[pb:pb + 64, (j2 // 2) * PT + c * L:
                               (j2 // 2) * PT + (c + 1) * L]
                    v_stt(YD2[pb:pb + 64, par * 128:(par + 1) * 128], hsl,
                          dbc[pb:pb + 64, j2:j2 + 1],
                          bk2[pb:pb + 64, 0:128], ALU.mult, ALU.add,
                          waits=[W("pe")])
                    v_stt(SG2[pb:pb + 64, par * 128:(par + 1) * 128],
                          TG2[pb:pb + 64, (j2 % 4) * 128:(j2 % 4 + 1) * 128],
                          1.0, gsl, ALU.add, ALU.mult)
                    v_stt(gsl, YD2[pb:pb + 64, par * 128:(par + 1) * 128],
                          0.5, SG2[pb:pb + 64, par * 128:(par + 1) * 128],
                          ALU.mult, ALU.mult)
                    v_stt(ST[:, j2 * 64:(j2 + 1) * 64],
                          ST[:, j2 * 64:(j2 + 1) * 64],
                          ES2[:, (j2 % 4) * 128 + 127:(j2 % 4) * 128 + 128],
                          bk2[:, 128:192],
                          ALU.mult, ALU.add, inc="v")

        if first:
            ddump("d_y", gate_p[:, :])
            ddump("d_st", ST[:, :])
        for eb in range(GB):
            s_act(sq2[1][:, :], gate_p[:, eb * PT:(eb + 1) * PT], AF.Square,
                  waits=[W("v"), W("pe")], inc="s")
            mm(pEr[0:1, :], ocb[:, :], sq2[1][:, :], eb == 0, eb == GB - 1,
               waits=[W("s")], inc="pe")
        ln_rows(INTER, 1e-5, False)
        bcast2(False)
        for eb in range(GB):
            v_stt(gate_p[:, eb * PT:(eb + 1) * PT],
                  gate_p[:, eb * PT:(eb + 1) * PT], nw_s[:, eb:eb + 1],
                  pA0[:, :], ALU.mult, ALU.mult,
                  waits=[W("pe")] if eb == 0 else (),
                  inc="v" if eb == GB - 1 else None)

        if first:
            ddump("d_yn", gate_p[:, :])
        def outp_dma(u, extra=()):
            if u >= OBK * 2:
                return
            ob, half = u // 2, u % 2
            src = dram["wout"].ap()[ob][:, half * 2048:(half + 1) * 2048]
            dma("sync", wf[u % 2][:, :], src, "w",
                waits=list(extra) + [W("s")])
            wafter[1000 + u] = C["w"]

        outp_dma(0)
        outp_dma(1)
        for ob in range(OBK):
            ps = pA0 if ob % 2 == 0 else pA1
            for half in range(2):
                u = ob * 2 + half
                wbb = wb[u % 2]
                s_cp(wbb[:, :],
                     wf[u % 2][:, :],
                     waits=[("w", wafter[1000 + u]), W("pe"),
                            ("a", C["a"])], inc="s")
                mmw = [W("s"), W("v")]
                for k in range(KD):
                    kk = half * KD + k
                    mm(ps[:, :], wbb[:, k * 128:(k + 1) * 128],
                       gate_p[:, kk * PT:(kk + 1) * PT], kk == 0, kk == 31,
                       waits=mmw if k == 0 else (),
                       inc="pe" if kk == 31 else None)
                outp_dma(u + 2)
            s_cp(attn_p[:, ob * PT:(ob + 1) * PT], ps[:, :],
                 waits=[W("pe")], inc="s")
        if first:
            ddump("d_attn", attn_p[:, :])
        for i in range(OBK):
            s_act(sq2[2][:, :], attn_p[:, i * PT:(i + 1) * PT], AF.Copy,
                  waits=[W("pe")] if i > 0 else [W("s")])
            s_act(sq2[3][:, :], attn_p[:, i * PT:(i + 1) * PT], AF.Square,
                  inc="s")
            mm(pTf[0:1, :], ocb[:, :], sq2[2][:, :], i == 0, i == OBK - 1,
               waits=[W("s")])
            mm(pEr[0:1, :], ocb[:, :], sq2[3][:, :], i == 0, i == OBK - 1,
               inc="pe")
        ln_rows(D, 1e-5, True)
        bcast2()
        for i in range(KD):
            xb = xb2[i % 2]
            dma("gp", xb[:, :],
                dram["xT"].ap()[i * 128:(i + 1) * 128, t0:t0 + PT], "a",
                waits=[W("v")])
            aw = ("a", C["a"])
            tf = t_f[i % 2]
            v_tt(tf[:, :], attn_p[:, i * PT:(i + 1) * PT], pA0[:, :],
                 ALU.mult, waits=[aw, W("pe")])
            v_tt(tf[:, :], tf[:, :], pA1[:, :], ALU.subtract)
            v_tt(tf[:, :], tf[:, :], xb[:, :], ALU.add, inc="v")
            dma("gp", x2s.ap()[i * 128:(i + 1) * 128, t0:t0 + PT], tf[:, :],
                "a", waits=[W("v")])
            s_act(sq2[2][:, :], tf[:, :], AF.Copy,
                  waits=[W("pe"), W("v")])
            s_act(sq2[3][:, :], tf[:, :], AF.Square, inc="s")
            mm(pTf[0:1, :], ocb[:, :], sq2[2][:, :], i == 0, i == KD - 1,
               waits=[W("s")])
            mm(pEr[0:1, :], ocb[:, :], sq2[3][:, :], i == 0, i == KD - 1,
               inc="pe")
        ln_rows(D, 1e-5, True)
        bcast2()
        x2w = ("a", C["a"])
        for i in range(KD):
            xb = xb2[i % 2]
            dma("gp", xb[:, :], x2s.ap()[i * 128:(i + 1) * 128, t0:t0 + PT],
                "a", waits=[x2w, W("v")])
            aw = ("a", C["a"])
            v_tt(t_f[0][:, :], xb[:, :], pA0[:, :], ALU.mult,
                 waits=[aw, W("pe")])
            v_tt(xn_p[:, i * PT:(i + 1) * PT], t_f[0][:, :], pA1[:, :],
                 ALU.subtract, inc="v")
        dma("gp",
            x2n.ap()[:, t0:t0 + PT].rearrange("(b p) t -> p b t", p=128),
            xn_p[:, :].rearrange("p (b t) -> p b t", t=PT), "a",
            waits=[W("v")])
        if first:
            ddump("d_x2n", xn_p[:, :])
        if first:
            while (a1_state["un"] < EBT or a1_state["gn"] < EBT or
                   a1_state["dn"] < EBT):
                emit_a1_unit()
            vlast = a1vtmp if (EBT - 1) % 2 == 0 else a1vtmp2
            v_tt(a1ad[:, :], a1ad[:, :], vlast[:, :], ALU.add, inc="v")
            mm(pTf[0:1, 0:1], a1au[:, :], ocf[:, :], True, True,
               waits=[W("s"), W("v"), W("pe")])
            mm(pTf[0:1, 1:2], a1ag[:, :], ocf[:, :], True, True)
            mm(pTf[0:1, 2:3], a1ad[:, :], ocf[:, :], True, True, inc="pe")
            v_cp(totals[0:1, 0:3], pTf[0:1, 0:3], waits=[W("pe")], inc="v")

    v_ts(row6[0:1, 6:7], totals[0:1, 0:1], 0.5 / NELEM_FFN, ALU.mult, 5e-9,
         ALU.add, waits=[W("v")])
    v_ts(row6[0:1, 0:3], totals[0:1, 0:3], 1.0 / NELEM_FFN, ALU.mult, 1e-8,
         ALU.add)
    v_ts(row6[0:1, 3:6], totals[0:1, 0:3], 0.5 / NELEM_FFN, ALU.mult, 5e-9,
         ALU.add)
    v_tt(row6[0:1, 7:8], row6[0:1, 6:7], row6[0:1, 1:2], ALU.mult, inc="v")
    mm(pEr[:, 0:8], o1f[:, :], row6[0:1, 0:8], True, True,
       waits=[W("v")], inc="pe")
    v_cp(cc[:, :], pEr[:, 0:8], waits=[W("pe")])
    v_ts(ncc[:, 0:3], pEr[:, 3:6], -1.0, ALU.mult, inc="v")
    ddump("d_cc", cc[:, :])

    xq = xn_p
    hq = attn_p
    accq = pool_a

    for q4 in range(FQ):
        t0 = q4 * FT
        firstq = q4 == 0
        dma("gp", xq[:, :].rearrange("p (b t) -> p b t", t=FT),
            x2n.ap()[:, t0:t0 + FT].rearrange("(b p) t -> p b t", p=128),
            "a", waits=[("a", C["a"]), W("v"), W("pe")])
        xqw = ("a", C["a"])
        if firstq:
            ddump("d_xq", xq[:, :])

        units = []
        for g in range(4):
            for eb in range(16):
                units.append(("u", g * 16 + eb))
                units.append(("g", g * 16 + eb))
            for ob in range(16):
                units.append(("d", ob * 4 + g))
        NU = len(units)
        qdr = {"u": (dram["wu"], qu), "g": (dram["wg"], qg),
               "d": (dram["wd"], qd)}
        hcol = {"u": cc[:, 3:4], "g": cc[:, 4:5], "d": cc[:, 5:6]}
        nhcol = {"u": ncc[:, 0:1], "g": ncc[:, 1:2], "d": ncc[:, 2:3]}

        def ffn_dma(u):
            if u >= NU:
                return
            kind, idx = units[u]
            f32d, qdram = qdr[kind]
            if firstq:
                dma("sync", wf[u % 2][:, :], f32d.ap()[idx], "w",
                    waits=[W("v")])
            else:
                dma("sync", wb[u % 2][:, :], qdram.ap()[idx], "w",
                    waits=[W("pe"), ("a", C["a"])])
            wafter[2000 + u] = C["w"]

        ffn_dma(0)
        ffn_dma(1)
        uidx = 0
        for g in range(4):
            for eb in range(16):
                for su in range(2):
                    u = uidx
                    uidx += 1
                    kind, idx = units[u]
                    wbb = wb[u % 2]
                    if firstq:
                        v_ts(qtmp[:, :], wf[u % 2][:, :], hcol[kind],
                             ALU.is_ge, -1.0, ALU.add,
                             waits=[("w", wafter[2000 + u]), W("pe"),
                                    ("a", C["a"])])
                        v_stt(wbb[:, :], wf[u % 2][:, :], nhcol[kind],
                              qtmp[:, :], ALU.is_ge, ALU.add, inc="v")
                        dma("gp", qdr[kind][1].ap()[idx], wbb[:, :], "a",
                            waits=[W("v")])
                        mmw = [W("v"), W("s"), xqw]
                    else:
                        mmw = [("w", wafter[2000 + u]), W("s"), W("v"), xqw]
                    ps = pA0 if su == 0 else pA1
                    for k in range(KD):
                        mm(ps[:, :], wbb[:, k * 128:(k + 1) * 128],
                           xq[:, k * FT:(k + 1) * FT], k == 0, k == KD - 1,
                           waits=mmw if k == 0 else (),
                           inc="pe" if k == KD - 1 else None)
                    ffn_dma(u + 2)
                pew = W("pe")
                if debug and firstq and g == 0 and eb == 0:
                    ddump("d_wq", wb[0][:, :])
                s_act(sq2[0][:, :], pA0[:, :], AF.Tanh, scale=cc[:, 6:7],
                      waits=[pew], inc="s")
                v_stt(t_f[0][:, :], sq2[0][:, :], 1.0, pA0[:, :], ALU.add,
                      ALU.mult, waits=[W("s")])
                v_stt(hq[:, eb * FT:(eb + 1) * FT], t_f[0][:, :], cc[:, 7:8],
                      pA1[:, :], ALU.mult, ALU.mult, inc="v")
            if debug and firstq and g == 0:
                ddump("d_hq", hq[:, :])
            for ob in range(16):
                u = uidx
                uidx += 1
                kind, idx = units[u]
                wbb = wb[u % 2]
                if firstq:
                    v_ts(qtmp[:, :], wf[u % 2][:, :], hcol[kind], ALU.is_ge,
                         -1.0, ALU.add,
                         waits=[("w", wafter[2000 + u]), W("pe"),
                                ("a", C["a"])])
                    v_stt(wbb[:, :], wf[u % 2][:, :], nhcol[kind], qtmp[:, :],
                          ALU.is_ge, ALU.add, inc="v")
                    dma("gp", qdr[kind][1].ap()[idx], wbb[:, :], "a",
                        waits=[W("v")])
                    mmw = [W("v")]
                else:
                    mmw = [("w", wafter[2000 + u]), W("v")]
                ps = pA0 if ob % 2 == 0 else pA1
                for k in range(KD):
                    mm(ps[:, :], wbb[:, k * 128:(k + 1) * 128],
                       hq[:, k * FT:(k + 1) * FT], k == 0, k == KD - 1,
                       waits=mmw if k == 0 else (),
                       inc="pe" if k == KD - 1 else None)
                ffn_dma(u + 2)
                pew = W("pe")
                if g == 0:
                    v_cp(accq[:, ob * FT:(ob + 1) * FT], ps[:, :],
                         waits=[pew], inc="v")
                else:
                    v_tt(accq[:, ob * FT:(ob + 1) * FT],
                         accq[:, ob * FT:(ob + 1) * FT], ps[:, :], ALU.add,
                         waits=[pew], inc="v")
            if debug and firstq and g == 0:
                ddump("d_acc0", accq[:, :])
        for i in range(OBK):
            s_act(sq2[2][:, :], accq[:, i * FT:(i + 1) * FT], AF.Copy,
                  scale=cc[:, 2:3],
                  waits=[W("pe"), W("v")] if i == 0 else [W("pe")])
            s_act(sq2[3][:, :], accq[:, i * FT:(i + 1) * FT], AF.Square,
                  scale=cc[:, 2:3], inc="s")
            mm(pTf[0:1, :], ocb[:, :], sq2[2][:, :], i == 0, i == OBK - 1,
               waits=[W("s")])
            mm(pEr[0:1, :], ocb[:, :], sq2[3][:, :], i == 0, i == OBK - 1,
               inc="pe")
        ln_rows(D, 1e-5, True)
        bcast2()
        for i in range(OBK):
            xb = xb2[i % 2]
            dma("gp", xb[:, :], x2s.ap()[i * 128:(i + 1) * 128, t0:t0 + FT],
                "a", waits=[W("v")])
            aw = ("a", C["a"])
            tf = obf[i % 2]
            v_stt(tf[:, :], accq[:, i * FT:(i + 1) * FT], cc[:, 2:3],
                  pA0[:, :], ALU.mult, ALU.mult, waits=[aw, W("pe")])
            v_tt(tf[:, :], tf[:, :], pA1[:, :], ALU.subtract)
            v_tt(tf[:, :], tf[:, :], xb[:, :], ALU.add, inc="v")
            dma("gp", out_t.ap()[i * 128:(i + 1) * 128, t0:t0 + FT],
                tf[:, :], "o", waits=[W("v")])

    E("gp", lambda: ENG["gp"].wait_ge(sem["o"], C["o"]))

    with nc.Block() as block:
        def replay(name):
            eng = ENG[name]
            for (waits, fn, inc, inc_n) in ops[name]:
                for (sm, val) in waits:
                    eng.wait_ge(sem[sm], val)
                inst = fn()
                if inc is not None:
                    inst.then_inc(sem[inc], inc_n)

        @block.sync
        def _(eng):
            ENG["sync"] = nc.sync
            replay("sync")

        @block.gpsimd
        def _(eng):
            ENG["gp"] = nc.gpsimd
            replay("gp")

        @block.scalar
        def _(eng):
            ENG["sc"] = nc.scalar
            replay("sc")

        @block.vector
        def _(eng):
            ENG["v"] = nc.vector
            replay("v")

        @block.tensor
        def _(eng):
            ENG["pe"] = nc.tensor
            replay("pe")

    return nc


def kernel(**inputs):
    f32 = np.float32
    xs = np.asarray(inputs["L0_attn_hidden_states"], f32)
    win = np.asarray(inputs["L0_attn_in_proj_w"], f32)
    convw = np.asarray(inputs["L0_attn_conv_w"], f32)
    convb = np.asarray(inputs["L0_attn_conv_b"], f32)
    dtb = np.asarray(inputs["L0_attn_dt_bias"], f32)
    alog = np.asarray(inputs["L0_attn_A_log"], f32)
    dvec = np.asarray(inputs["L0_attn_D"], f32)
    nw = np.asarray(inputs["L0_attn_norm_weight"], f32)
    wout = np.asarray(inputs["L0_attn_out_proj_w"], f32)
    wu = np.asarray(inputs["L0_ffn_W_up"], f32)
    wg = np.asarray(inputs["L0_ffn_W_gate"], f32)
    wd = np.asarray(inputs["L0_ffn_W_down"], f32)

    ct = np.ascontiguousarray
    WinT = ct(win.T)
    wing = ct(WinT[:, :4096].reshape(16, 128, 32, 128)
              .transpose(2, 1, 0, 3)).reshape(32, 128, 2048)
    winh = ct(WinT[:, 4096:8448].reshape(16, 128, 34, 128)
              .transpose(2, 1, 0, 3)).reshape(34, 128, 2048)
    wind = ct(WinT[:, 8448:8512].reshape(16, 128, 64)
              .transpose(1, 0, 2)).reshape(128, 1024)
    wib_host = np.concatenate([wing, winh], axis=0)
    WoT = ct(wout.T)
    woutt = ct(WoT.reshape(32, 128, 16, 128)
               .transpose(2, 1, 0, 3)).reshape(16, 128, 4096)
    wut = ct(ct(wu.T).reshape(16, 128, 64, 128)
             .transpose(2, 1, 0, 3)).reshape(64, 128, 2048)
    wgt = ct(ct(wg.T).reshape(16, 128, 64, 128)
             .transpose(2, 1, 0, 3)).reshape(64, 128, 2048)
    wdt = ct(ct(wd.T).reshape(4, 16, 128, 16, 128)
             .transpose(3, 0, 2, 1, 4)).reshape(64, 128, 2048)
    convwt = ct(convw[:, 0, :].reshape(34, 128, 4)
                .transpose(1, 0, 2)).reshape(128, 136)
    convbt = ct(convb.reshape(34, 128).T)
    nwt = ct(nw.reshape(32, 128).T)
    cum = np.ascontiguousarray(np.triu(np.ones((128, 128), f32)))
    identf = np.eye(128, dtype=f32)
    identb = np.eye(128, dtype=f32).astype(ml_dtypes.bfloat16)
    ones1f = np.ones((1, 128), f32)
    ones1b = np.ones((1, 128), f32).astype(ml_dtypes.bfloat16)
    onescb = np.ones((128, 1), f32).astype(ml_dtypes.bfloat16)
    onescf = np.ones((128, 1), f32)
    del wib_host

    nc = build_program()
    in_maps = []
    for b in range(8):
        m = {
            "xT": ct(xs[b].T),
            "wing": wing, "winh": winh, "wind": wind, "wout": woutt,
            "wu": wut, "wg": wgt, "wd": wdt,
            "a1u": wut, "a1g": wgt, "a1d": wdt,
            "convw": convwt, "convb": convbt, "nw": nwt,
            "dtb": ct(dtb.reshape(64, 1)), "alog": ct(alog.reshape(64, 1)),
            "dvec": ct(dvec.reshape(1, 64)),
            "cumf": cum, "identf": identf, "identb": identb,
            "ones1f": ones1f, "ones1b": ones1b,
            "onescb": onescb, "onescf": onescf,
        }
        in_maps.append(m)
    res = run_bass_kernel_spmd(nc, in_maps, core_ids=list(range(8)))
    out = np.empty((8, 2048, 2048), f32)
    for b in range(8):
        out[b] = np.asarray(res.results[b]["out"], f32).T
    return out
